# version 02
# baseline (speedup 1.0000x reference)
import numpy as np
import ml_dtypes
from contextlib import ExitStack

import concourse.bass as bass
import concourse.mybir as mybir
from concourse.bass_utils import run_bass_kernel_spmd

F32 = mybir.dt.float32
BF16 = mybir.dt.bfloat16
I32 = mybir.dt.int32
AF = mybir.ActivationFunctionType
ALU = mybir.AluOpType
AX = mybir.AxisListType

T = 8192
C = 1024
D = 64
TS = 2048
HID = 3072
QC = 512
NQC = T // QC
NKT = T // 128
S127 = float(np.float32(1.0 / 127.0))
SCALE = 0.125
DEBUG = False
STAGES = {"QKV": True, "B": True, "B2": True, "C": True, "D": True}


class Sched:
    ENGINES = ("sync", "tensor", "vector", "scalar", "gpsimd")

    def __init__(self, nc, ctx):
        self.nc = nc
        self.q = {e: [] for e in self.ENGINES}
        self.sems = {}
        self.cnt = {}
        for name in ("ds", "da", "dp", "pe", "dv", "ac", "cc"):
            self.sems[name] = ctx.enter_context(nc.semaphore(name))
            self.cnt[name] = 0

    def add(self, eng, fn, inc=(), wait=()):
        w = [(n, v) for (n, v) in wait if v > 0]
        for n, v in w:
            assert v <= self.cnt[n], f"wait on future: {n} {v} > {self.cnt[n]}"
        self.q[eng].append((w, fn, list(inc)))
        for n, v in inc:
            self.cnt[n] += v

    def dma(self, eng, out, in_, wait=()):
        sem = {"sync": "ds", "scalar": "da", "gpsimd": "dp"}[eng]
        self.add(eng, lambda q, out=out, in_=in_: q.dma_start(out=out, in_=in_),
                 inc=[(sem, 16)], wait=wait)
        return self.cnt[sem]

    def dmaT(self, eng, out, in_, wait=()):
        sem = {"sync": "ds", "scalar": "da"}[eng]
        self.add(eng, lambda q, out=out, in_=in_: q.dma_start_transpose(out, in_),
                 inc=[(sem, 16)], wait=wait)
        return self.cnt[sem]

    def emit(self, block):
        for eng in self.ENGINES:
            ops = self.q[eng]
            sems = self.sems

            def mk(ops):
                def go(q):
                    hi = {}
                    for w, fn, i in ops:
                        for n, v in w:
                            if v > hi.get(n, 0):
                                q.wait_ge(sems[n], v)
                                hi[n] = v
                        instr = fn(q)
                        for n, v in i:
                            instr.then_inc(sems[n], v)
                return go
            getattr(block, eng)(mk(ops))


def build_nc():
    nc = bass.Bass("TRN2", target_bir_lowering=False)

    x_full = nc.dram_tensor("x_full", [T, C], F32, kind="ExternalInput")
    x_own = nc.dram_tensor("x_own", [TS, C], F32, kind="ExternalInput")
    wqkv = nc.dram_tensor("wqkv", [C, 768], F32, kind="ExternalInput")
    bqkv = nc.dram_tensor("bqkv", [128, 6], F32, kind="ExternalInput")
    wproj = nc.dram_tensor("wproj", [256, C], F32, kind="ExternalInput")
    bproj = nc.dram_tensor("bproj", [128, 8], F32, kind="ExternalInput")
    wgT = nc.dram_tensor("wgT", [C, HID], I32, kind="ExternalInput")
    wuT = nc.dram_tensor("wuT", [C, HID], I32, kind="ExternalInput")
    wdT = nc.dram_tensor("wdT", [HID, C], I32, kind="ExternalInput")
    ident = nc.dram_tensor("ident", [128, 128], BF16, kind="ExternalInput")
    maskw = nc.dram_tensor("maskw", [128, 896], BF16, kind="ExternalInput")
    ones1 = nc.dram_tensor("ones1", [1, 128], F32, kind="ExternalInput")
    out_own = nc.dram_tensor("out_own", [TS, C], F32, kind="ExternalOutput")

    dbg = "ExternalOutput" if DEBUG else "Internal"
    xn_dram = nc.dram_tensor("xn_dram", [T, C], BF16, kind=dbg)
    xn_own_dram = nc.dram_tensor("xn_own_dram", [TS, C], BF16, kind="Internal")
    h_dram = nc.dram_tensor("h_dram", [HID, TS], BF16, kind=dbg)
    g_dram = nc.dram_tensor("g_dram", [C, TS], F32, kind=dbg)
    p_dram = nc.dram_tensor("p_dram", [C, TS], BF16, kind="Internal")
    wg_bf = nc.dram_tensor("wg_bf", [C, HID], BF16, kind="Internal")
    wu_bf = nc.dram_tensor("wu_bf", [C, HID], BF16, kind="Internal")
    wd_bf = nc.dram_tensor("wd_bf", [HID, C], BF16, kind="Internal")
    rs_in = nc.dram_tensor("rs_in", [4 * C, TS], F32, kind="Internal")
    rs_out = nc.dram_tensor("rs_out", [C, TS], F32, kind="Internal")
    if DEBUG:
        qt_dbg = nc.dram_tensor("qt_dbg", [128, 16384], BF16, kind="ExternalOutput")
        kt_dbg = nc.dram_tensor("kt_dbg", [128, 16384], BF16, kind="ExternalOutput")
        vt_dbg = nc.dram_tensor("vt_dbg", [128, 16640], BF16, kind="ExternalOutput")
        y_dbg = nc.dram_tensor("y_dbg", [128, 16384], BF16, kind="ExternalOutput")
        rsin_dbg = nc.dram_tensor("rsin_dbg", [4 * C, TS], F32, kind="ExternalOutput")
        wgu_dbg = nc.dram_tensor("wgu_dbg", [128, 6144], BF16, kind="ExternalOutput")
        sg_dbg = nc.dram_tensor("sg_dbg", [128, 2048], F32, kind="ExternalOutput")
        rsout_dbg = nc.dram_tensor("rsout_dbg", [C, TS], F32, kind="ExternalOutput")

    BASE = 24576
    TOP = 229344
    allocs = []

    def sb(name, cols, dtype, off, ph0, ph1):
        size = cols * (2 if dtype in (BF16,) else 4)
        assert off % 32 == 0 and off >= BASE and off + size <= TOP, (name, off, size)
        for (n2, o2, s2, p0, p1) in allocs:
            if not (ph1 < p0 or p1 < ph0):
                assert off + size <= o2 or o2 + s2 <= off, f"overlap {name} vs {n2}"
        allocs.append((name, off, size, ph0, ph1))
        return nc.alloc_sbuf_tensor_at(name, [128, cols], dtype, offset=off)

    s_ident = sb("s_ident", 128, BF16, 24576, 0, 4)
    s_maskw = sb("s_maskw", 896, BF16, 24832, 0, 4)
    s_ones = nc.alloc_sbuf_tensor_at("s_ones", [1, 128], F32, offset=26624)
    allocs.append(("s_ones", 26624, 512, 0, 4))
    s_bqkv = sb("s_bqkv", 8, F32, 27136, 0, 4)
    s_bproj = sb("s_bproj", 8, F32, 27168, 0, 4)
    s_stats = sb("s_stats", 24, F32, 27200, 0, 4)
    s_xnt_own = sb("s_xnt_own", 16384, BF16, 27328, 0, 4)
    s_wprojb = sb("s_wprojb", 2048, BF16, 60096, 0, 4)

    s_QT = sb("s_QT", 16384, BF16, 64192, 0, 2)
    s_KT = sb("s_KT", 16384, BF16, 96960, 0, 2)
    s_VT = sb("s_VT", 16640, BF16, 129728, 0, 2)
    s_ySB = sb("s_ySB", 16384, BF16, 163008, 1, 2)

    s_xbuf = sb("s_xbuf", 4096, F32, 163008, 0, 0)
    s_sqbuf = sb("s_sqbuf", 1024, F32, 179392, 0, 0)
    s_xnst = sb("s_xnst", 2048, BF16, 183488, 0, 0)
    s_xntb = sb("s_xntb", 8192, BF16, 187584, 0, 0)
    s_wqkvb = sb("s_wqkvb", 6144, BF16, 203968, 0, 0)
    s_vstage = sb("s_vstage", 2048, BF16, 216256, 0, 0)

    s_ptb = sb("s_ptb", 4096, BF16, 195776, 1, 2)
    s_deninv = sb("s_deninv", 1024, F32, 203968, 1, 2)
    s_ycp = sb("s_ycp", 1024, F32, 208064, 1, 2)
    s_spstg = sb("s_spstg", 1024, F32, 212160, 1, 2)

    s_wdb = sb("s_wdb", 24576, BF16, 64192, 3, 3)
    s_wgu = sb("s_wgu", 6144, BF16, 113344, 3, 3)
    s_sg = sb("s_sg", 2048, F32, 125632, 3, 3)
    s_hhh = sb("s_hhh", 4096, BF16, 133824, 3, 3)
    s_ht = sb("s_ht", 24576, BF16, 142016, 3, 3)
    s_gstg = sb("s_gstg", 1024, F32, 191168, 3, 3)

    s_s2 = sb("s_s2", 1024, F32, 64192, 4, 4)
    s_g2 = sb("s_g2", 1024, F32, 68288, 4, 4)
    s_pt = sb("s_pt", 1024, BF16, 72384, 4, 4)
    s_ptT = sb("s_ptT", 2048, BF16, 74432, 4, 4)
    s_xo = sb("s_xo", 2048, F32, 78528, 4, 4)
    s_outt = sb("s_outt", 2048, F32, 86720, 4, 4)

    with ExitStack() as ctx:
        main = ctx.enter_context(nc.psum_tensor("main", [128, 3072], F32))
        ptA = ctx.enter_context(nc.psum_tensor("ptA", [128, 1024], BF16))
        ptB = ctx.enter_context(nc.psum_tensor("ptB", [128, 1024], BF16))
        s = Sched(nc, ctx)
        block = ctx.enter_context(nc.Block())

        s.dma("sync", s_ident[:], ident[:, :])
        s.dma("sync", s_maskw[:], maskw[:, :])
        s.dma("sync", s_ones[:], ones1[:, :])
        s.dma("sync", s_bqkv[:, 0:6], bqkv[:, :])
        ds_consts = s.dma("sync", s_bproj[:, 0:8], bproj[:, :])
        s.dma("gpsimd", s_wqkvb[:].rearrange("p (a d) -> p a d", d=768),
              wqkv.ap().rearrange("(a p) d -> p a d", p=128))
        dp_wqkv = s.cnt["dp"]
        s.dma("gpsimd", s_wprojb[:].rearrange("p (a d) -> p a d", d=1024),
              wproj.ap().rearrange("(a p) d -> p a d", p=128))
        dp_wproj = s.cnt["dp"]
        for half in range(2):
            s.dma("gpsimd", wg_bf.ap()[half * 512:(half + 1) * 512, :],
                  wgT.ap()[half * 512:(half + 1) * 512, :])
            s.dma("gpsimd", wu_bf.ap()[half * 512:(half + 1) * 512, :],
                  wuT.ap()[half * 512:(half + 1) * 512, :])
            s.dma("gpsimd", wd_bf.ap()[half * 1536:(half + 1) * 1536, :],
                  wdT.ap()[half * 1536:(half + 1) * 1536, :])
        dp_wbf = s.cnt["dp"]
        s.add("vector", lambda q: q.memset(s_VT[:], 1.0), inc=[("dv", 1)])
        dv_vtones = s.cnt["dv"]

        xcons_dv = [0, 0, 0, 0]
        xcons_ac = [0, 0, 0, 0]
        xnst_ds = [0, 0]
        stats_free_ac = [0, 0]
        x_loaded_ds = [0, 0, 0, 0]

        def ln_tile(ti, src_rows, dst_dram, dst_row):
            slot = ti % 4
            sb_ = (ti % 2) * 10
            st = s_stats
            a_s1 = st[:, sb_ + 0:sb_ + 1]
            a_s12 = st[:, sb_ + 0:sb_ + 2]
            a_s2 = st[:, sb_ + 1:sb_ + 2]
            a_mex = st[:, sb_ + 2:sb_ + 4]
            a_m = st[:, sb_ + 2:sb_ + 3]
            a_ex2 = st[:, sb_ + 3:sb_ + 4]
            a_nvar = st[:, sb_ + 5:sb_ + 6]
            a_sd = st[:, sb_ + 6:sb_ + 7]
            a_rstd = st[:, sb_ + 7:sb_ + 8]
            a_negb = st[:, sb_ + 8:sb_ + 9]
            x_loaded_ds[slot] = s.dma(
                "sync", s_xbuf[:, slot * 1024:(slot + 1) * 1024], src_rows,
                wait=[("dv", xcons_dv[slot]), ("ac", xcons_ac[slot])])
            xs = s_xbuf[:, slot * 1024:(slot + 1) * 1024]
            s.add("vector", lambda q, xs=xs, o=a_s1: q.reduce_sum(o, xs, axis=AX.X),
                  inc=[("dv", 1)],
                  wait=[("ds", x_loaded_ds[slot]), ("ac", stats_free_ac[ti % 2])])
            dv_red = s.cnt["dv"]
            s.add("scalar", lambda q, xs=xs, o=a_s2: q.activation(
                s_sqbuf[:], xs, AF.Square, accum_out=o),
                inc=[("ac", 1)], wait=[("ds", x_loaded_ds[slot])])
            ac_sq = s.cnt["ac"]
            s.add("vector", lambda q, o=a_mex, i_=a_s12: q.tensor_scalar_mul(
                o, i_, 1.0 / 1024.0),
                inc=[("dv", 1)], wait=[("ac", ac_sq), ("dv", dv_red)])
            dv_mex = s.cnt["dv"]
            s.add("vector", lambda q, o=a_nvar, i_=a_m, e_=a_ex2: q.scalar_tensor_tensor(
                o, i_, i_, e_, ALU.mult, ALU.subtract),
                inc=[("dv", 1)], wait=[("dv", dv_mex)])
            dv_var = s.cnt["dv"]
            s.add("scalar", lambda q, o=a_sd, i_=a_nvar: q.activation(
                o, i_, AF.Sqrt, scale=-1.0, bias=np.float32(1e-5)),
                inc=[("ac", 1)], wait=[("dv", dv_var)])
            ac_sd = s.cnt["ac"]
            s.add("vector", lambda q, o=a_rstd, i_=a_sd: q.reciprocal(o, i_),
                  inc=[("dv", 1)], wait=[("ac", ac_sd)])
            dv_rstd = s.cnt["dv"]
            s.add("vector", lambda q, o=a_negb, i_=a_m, r_=a_rstd: q.scalar_tensor_tensor(
                o, i_, -1.0, r_, ALU.mult, ALU.mult),
                inc=[("dv", 1)], wait=[("dv", dv_rstd)])
            dv_nb = s.cnt["dv"]
            xcons_dv[slot] = dv_nb
            xo = s_xnst[:, (ti % 2) * 1024:(ti % 2 + 1) * 1024]
            s.add("scalar", lambda q, xs=xs, xo=xo, sc_=a_rstd, b_=a_negb: q.activation(
                xo, xs, AF.Identity, scale=sc_, bias=b_),
                inc=[("ac", 1)], wait=[("dv", dv_nb), ("ds", xnst_ds[ti % 2])])
            xcons_ac[slot] = s.cnt["ac"]
            stats_free_ac[ti % 2] = s.cnt["ac"]
            xnst_ds[ti % 2] = s.dma("sync", dst_dram[dst_row:dst_row + 128, :], xo,
                                    wait=[("ac", s.cnt["ac"])])
            return xnst_ds[ti % 2]

        xntb_pe = [0, 0]
        qkvps_dv = [0, 0]
        vstage_pe = 0
        ptps_dv = [dv_vtones, dv_vtones]
        n_qkv = [0]

        def qkv_chunk3(sc, ds_ready):
            nonlocal vstage_pe
            da = 0
            for ch in range(8):
                da = s.dmaT("scalar",
                            s_xntb[:, ch * 1024:(ch + 1) * 1024],
                            xn_dram.ap()[sc * 1024:(sc + 1) * 1024,
                                         ch * 128:(ch + 1) * 128],
                            wait=[("ds", ds_ready), ("pe", xntb_pe[0])])
            for dp_ in range(6):
                for tcc in range(2):
                    k = n_qkv[0] % 2
                    n_qkv[0] += 1
                    ps = main[:, k * 512:(k + 1) * 512]
                    for ch in range(8):
                        w = [("da", da), ("dp", dp_wqkv), ("dv", qkvps_dv[k])] if ch == 0 else []
                        st_ap = s_wqkvb[:, ch * 768 + dp_ * 128: ch * 768 + (dp_ + 1) * 128]
                        rhs = s_xntb[:, ch * 1024 + tcc * 512: ch * 1024 + tcc * 512 + 512]
                        s.add("tensor", lambda q, ps=ps, st_ap=st_ap, rhs=rhs, ch=ch:
                              q.matmul(ps, st_ap, rhs, start=(ch == 0), stop=(ch == 7)),
                              inc=[("pe", 1)], wait=w)
                    pe_mm = s.cnt["pe"]
                    xntb_pe[0] = pe_mm
                    t0 = sc * 1024 + tcc * 512
                    if dp_ < 2:
                        dst = s_QT[:, dp_ * 8192 + t0: dp_ * 8192 + t0 + 512]
                    elif dp_ < 4:
                        dst = s_KT[:, (dp_ - 2) * 8192 + t0: (dp_ - 2) * 8192 + t0 + 512]
                    else:
                        dst = s_vstage[:, (dp_ - 4) * 1024 + tcc * 512:
                                       (dp_ - 4) * 1024 + tcc * 512 + 512]
                    bias_ap = s_bqkv[:, dp_:dp_ + 1]
                    vw = [("pe", pe_mm)]
                    if dp_ >= 4:
                        vw.append(("pe", vstage_pe))
                    s.add("vector", lambda q, dst=dst, ps=ps, b=bias_ap:
                          q.tensor_scalar(dst, ps, b, None, ALU.add),
                          inc=[("dv", 1)], wait=vw)
                    qkvps_dv[k] = s.cnt["dv"]
                    if dp_ >= 4:
                        pair = dp_ - 4
                        dv_evac = s.cnt["dv"]
                        for i in range(4):
                            kt = sc * 8 + tcc * 4 + i
                            pt_ps = (ptA if (kt % 2 == 0) else ptB)[:, 0:128]
                            vsl = s_vstage[:, pair * 1024 + tcc * 512 + i * 128:
                                           pair * 1024 + tcc * 512 + (i + 1) * 128]
                            s.add("tensor", lambda q, o=pt_ps, i_=vsl:
                                  q.transpose(o, i_, s_ident[:]),
                                  inc=[("pe", 1)],
                                  wait=[("dv", dv_evac), ("dv", ptps_dv[kt % 2]),
                                        ("ds", ds_consts)])
                            pe_tr = s.cnt["pe"]
                            base = kt * 260 + 130 * pair
                            dst2 = s_VT[:, base:base + 130].rearrange(
                                "p (h x) -> p h x", x=65)[:, 0:2, 0:64]
                            src2 = pt_ps.rearrange("p (h x) -> p h x", x=64)
                            s.add("vector", lambda q, d=dst2, s2=src2: q.tensor_copy(d, s2),
                                  inc=[("dv", 1)], wait=[("pe", pe_tr)])
                            ptps_dv[kt % 2] = s.cnt["dv"]
                        vstage_pe = s.cnt["pe"]

        for sc in range(8):
            ds_last = 0
            for tt in range(8):
                ti = sc * 8 + tt
                ds_last = ln_tile(ti, x_full.ap()[ti * 128:(ti + 1) * 128, :],
                                  xn_dram, ti * 128)
            if STAGES["QKV"]:
                qkv_chunk3(sc, ds_last)

        ds_own = 0
        for tt in range(16):
            ds_own = ln_tile(64 + tt, x_own.ap()[tt * 128:(tt + 1) * 128, :],
                             xn_own_dram, tt * 128)
        for ch in range(8):
            s.dmaT("scalar", s_xnt_own[:, ch * 2048:(ch + 1) * 2048],
                   xn_own_dram.ap()[:, ch * 128:(ch + 1) * 128],
                   wait=[("ds", ds_own)])
        da_xnt_own = s.cnt["da"]

        dv_A_end = s.cnt["dv"]
        if DEBUG:
            s.dma("sync", qt_dbg.ap()[:, :], s_QT[:], wait=[("dv", dv_A_end)])
            s.dma("sync", kt_dbg.ap()[:, :], s_KT[:], wait=[("dv", dv_A_end)])
            s.dma("sync", vt_dbg.ap()[:, :], s_VT[:], wait=[("dv", dv_A_end)])
        last_exp_ac = [0, 0]
        last_av_pe = [0, 0]
        last_bc_dv = dv_A_end
        last_av_cons_dv = dv_A_end

        stb = [main[:, 0:1024], main[:, 1024:2048]]
        avps = main[:, 2048:3072]

        for qc in range(NQC if STAGES["B"] else 0):
            for pair in range(2):
                K = 4 * qc + 4
                for kt in range(K):
                    b_ = kt % 2
                    w = [("ac", last_exp_ac[b_]), ("dv", dv_A_end)]
                    if kt <= 1:
                        w.append(("dv", last_bc_dv))
                    stA = s_KT[0:64, pair * 8192 + kt * 128: pair * 8192 + (kt + 1) * 128]
                    stB = s_KT[64:128, pair * 8192 + kt * 128: pair * 8192 + (kt + 1) * 128]
                    rhA = s_QT[0:64, pair * 8192 + qc * 512: pair * 8192 + (qc + 1) * 512]
                    rhB = s_QT[64:128, pair * 8192 + qc * 512: pair * 8192 + (qc + 1) * 512]
                    s.add("tensor", lambda q, o=stb[b_][:, 0:512], a=stA, r=rhA:
                          q.matmul(o, a, r, start=True, stop=True),
                          inc=[("pe", 1)], wait=w)
                    s.add("tensor", lambda q, o=stb[b_][:, 512:1024], a=stB, r=rhB:
                          q.matmul(o, a, r, start=True, stop=True, tile_position=(64, 0)),
                          inc=[("pe", 1)])
                    pe_qk = s.cnt["pe"]
                    pt = s_ptb[:, b_ * 1024:(b_ + 1) * 1024]
                    s.add("scalar", lambda q, o=pt, i_=stb[b_][:, 0:1024]:
                          q.activation(o, i_, AF.Exp, scale=SCALE),
                          inc=[("ac", 1)],
                          wait=[("pe", pe_qk), ("pe", last_av_pe[b_])])
                    last_exp_ac[b_] = s.cnt["ac"]
                    ac_exp = s.cnt["ac"]
                    avw = [("ac", ac_exp)]
                    if kt >= 4 * qc:
                        o = kt * 128 - qc * 512
                        msk = s_maskw[:, 384 - o: 896 - o]
                        s.add("vector", lambda q, p0=pt[:, 0:512], m=msk:
                              q.tensor_mul(p0, p0, m), inc=[("dv", 1)],
                              wait=[("ac", ac_exp)])
                        s.add("vector", lambda q, p1=pt[:, 512:1024], m=msk:
                              q.tensor_mul(p1, p1, m), inc=[("dv", 1)])
                        avw.append(("dv", s.cnt["dv"]))
                    if kt == 0:
                        avw.append(("dv", last_av_cons_dv))
                    vA = s_VT[:, kt * 260 + 130 * pair: kt * 260 + 130 * pair + 65]
                    vB = s_VT[:, kt * 260 + 130 * pair + 65: kt * 260 + 130 * pair + 130]
                    s.add("tensor", lambda q, v=vA, p=pt[:, 0:512], st_=(kt == 0), sp=(kt == K - 1):
                          q.matmul(avps[0:65, 0:512], v, p, start=st_, stop=sp),
                          inc=[("pe", 1)], wait=avw)
                    s.add("tensor", lambda q, v=vB, p=pt[:, 512:1024], st_=(kt == 0), sp=(kt == K - 1):
                          q.matmul(avps[0:65, 512:1024], v, p, start=st_, stop=sp),
                          inc=[("pe", 1)])
                    last_av_pe[b_] = s.cnt["pe"]
                pe_av_done = s.cnt["pe"]
                s.add("vector", lambda q: q.reciprocal(
                    s_deninv[0:1, 0:512], avps[64:65, 0:512]),
                    inc=[("dv", 1)], wait=[("pe", pe_av_done)])
                s.add("vector", lambda q: q.reciprocal(
                    s_deninv[0:1, 512:1024], avps[64:65, 512:1024]),
                    inc=[("dv", 1)])
                dv_rec = s.cnt["dv"]
                s.add("vector", lambda q: q.tensor_copy(
                    s_ycp[0:64, 0:512], avps[0:64, 0:512]), inc=[("dv", 1)])
                s.add("vector", lambda q: q.tensor_copy(
                    s_ycp[64:128, 0:512], avps[0:64, 512:1024]), inc=[("dv", 1)])
                last_av_cons_dv = s.cnt["dv"]
                dv_ycp = s.cnt["dv"]
                s.add("tensor", lambda q: q.matmul(
                    stb[0][:, 0:512], s_ones[0:1, :], s_deninv[0:1, 0:512],
                    start=True, stop=True),
                    inc=[("pe", 1)],
                    wait=[("dv", dv_rec), ("ac", s.cnt["ac"])])
                s.add("tensor", lambda q: q.matmul(
                    stb[0][:, 512:1024], s_ones[0:1, :], s_deninv[0:1, 512:1024],
                    start=True, stop=True), inc=[("pe", 1)])
                pe_bc = s.cnt["pe"]
                yslA = s_ySB[0:64, pair * 8192 + qc * 512: pair * 8192 + (qc + 1) * 512]
                yslB = s_ySB[64:128, pair * 8192 + qc * 512: pair * 8192 + (qc + 1) * 512]
                s.add("vector", lambda q, o=yslA, i0=s_ycp[0:64, 0:512],
                      i1=stb[0][0:64, 0:512]: q.tensor_mul(o, i0, i1),
                      inc=[("dv", 1)], wait=[("pe", pe_bc), ("dv", dv_ycp)])
                s.add("vector", lambda q, o=yslB, i0=s_ycp[64:128, 0:512],
                      i1=stb[0][64:128, 512:1024]: q.tensor_mul(o, i0, i1),
                      inc=[("dv", 1)])
                last_bc_dv = s.cnt["dv"]

        dv_B_end = s.cnt["dv"]
        pe_B_end = s.cnt["pe"]
        b2_dv = [0, 0]
        b2_ds = [0, 0]
        for tcq in range(NQC if STAGES["B2"] else 0):
            for cp in range(8):
                k = (tcq * 8 + cp) % 2
                ps = main[:, k * 512:(k + 1) * 512]
                for pr in range(2):
                    w = [("dv", max(b2_dv[k], dv_B_end)), ("dp", dp_wproj)] if pr == 0 else []
                    st_ap = s_wprojb[:, pr * 1024 + cp * 128: pr * 1024 + (cp + 1) * 128]
                    rhs = s_ySB[:, pr * 8192 + tcq * 512: pr * 8192 + (tcq + 1) * 512]
                    s.add("tensor", lambda q, ps=ps, a=st_ap, r=rhs, pr=pr:
                          q.matmul(ps, a, r, start=(pr == 0), stop=(pr == 1)),
                          inc=[("pe", 1)], wait=w)
                pe_mm = s.cnt["pe"]
                stg = s_spstg[:, k * 512:(k + 1) * 512]
                s.add("vector", lambda q, o=stg, i_=ps: q.tensor_copy(o, i_),
                      inc=[("dv", 1)], wait=[("pe", pe_mm), ("ds", b2_ds[k])])
                b2_dv[k] = s.cnt["dv"]
                rb = 1024 * (tcq // 4) + cp * 128
                cb = (tcq % 4) * 512
                b2_ds[k] = s.dma("sync", rs_in.ap()[rb:rb + 128, cb:cb + 512], stg,
                                 wait=[("dv", b2_dv[k])])
        if DEBUG:
            s.dma("sync", y_dbg.ap()[:, :], s_ySB[:], wait=[("dv", s.cnt["dv"])])
            s.dma("sync", rsin_dbg.ap()[:, :], rs_in.ap()[:, :],
                  wait=[("ds", s.cnt["ds"])])
        ds_rs_in = s.cnt["ds"]
        dv_b2_end = s.cnt["dv"]

        wgu_da = []
        for hh in range(2 if STAGES["C"] else 0):
            for wt, wsrc in ((0, wg_bf), (1, wu_bf)):
                s.dma("scalar",
                      s_wgu[:, (hh % 3) * 2048 + wt * 1024:
                            (hh % 3) * 2048 + wt * 1024 + 1024]
                      .rearrange("p (a h) -> p a h", h=128),
                      wsrc.ap()[:, hh * 128:(hh + 1) * 128]
                      .rearrange("(a p) h -> p a h", p=128),
                      wait=[("pe", pe_B_end), ("dp", dp_wbf)])
            wgu_da.append(s.cnt["da"])
        if STAGES["B2"]:
            s.add("gpsimd", lambda q: q.collective_compute(
            "ReduceScatter", ALU.add,
                replica_groups=[[0, 1, 2, 3], [4, 5, 6, 7]],
                ins=[rs_in.ap().opt()], outs=[rs_out.ap().opt()]),
                inc=[("cc", 1)], wait=[("ds", ds_rs_in)])
        for part in range(4 if STAGES["C"] else 0):
            s.dma("scalar",
                  s_wdb[:].rearrange("p (a c) -> p a c", c=1024)[:, part * 6:(part + 1) * 6, :],
                  wd_bf.ap().rearrange("(a p) c -> p a c", p=128)[:, part * 6:(part + 1) * 6, :],
                  wait=[("pe", pe_B_end), ("dp", dp_wbf)])
        da_wd = s.cnt["da"]

        gate_bank_ac = [0, 0]
        up_bank_dv = [0, 0]
        sg_dv = [0, 0, 0, 0]
        hhh_ds = [0, 0]
        gu_pe = [0, 0, 0]
        hh_store_ds = 0
        for hh in range(24 if STAGES["C"] else 0):
            if hh >= 2:
                for wt, wsrc in ((0, wg_bf), (1, wu_bf)):
                    s.dma("scalar",
                          s_wgu[:, (hh % 3) * 2048 + wt * 1024:
                                (hh % 3) * 2048 + wt * 1024 + 1024]
                          .rearrange("p (a h) -> p a h", h=128),
                          wsrc.ap()[:, hh * 128:(hh + 1) * 128]
                          .rearrange("(a p) h -> p a h", p=128),
                          wait=[("pe", gu_pe[hh % 3])])
                wgu_da.append(s.cnt["da"])
            if DEBUG and hh == 2:
                s.dma("sync", wgu_dbg.ap()[:, :], s_wgu[:],
                      wait=[("da", wgu_da[2]), ("dv", dv_b2_end)])
            hb = hh % 2
            dv_last_stt = 0
            for tc in range(4):
                tb = tc % 2
                pg = main[:, tb * 512:(tb + 1) * 512]
                pu = main[:, 1024 + tb * 512: 1024 + (tb + 1) * 512]
                for ch in range(8):
                    w = [("da", max(wgu_da[hh], da_xnt_own)), ("ac", gate_bank_ac[tb]),
                         ("dv", dv_b2_end)] if ch == 0 else []
                    st_ap = s_wgu[:, (hh % 3) * 2048 + ch * 128:(hh % 3) * 2048 + (ch + 1) * 128]
                    rhs = s_xnt_own[:, ch * 2048 + tc * 512: ch * 2048 + (tc + 1) * 512]
                    s.add("tensor", lambda q, ps=pg, a=st_ap, r=rhs, ch=ch:
                          q.matmul(ps, a, r, start=(ch == 0), stop=(ch == 7)),
                          inc=[("pe", 1)], wait=w)
                pe_g = s.cnt["pe"]
                for ch in range(8):
                    w = [("dv", up_bank_dv[tb])] if ch == 0 else []
                    st_ap = s_wgu[:, (hh % 3) * 2048 + 1024 + ch * 128:
                                  (hh % 3) * 2048 + 1024 + (ch + 1) * 128]
                    rhs = s_xnt_own[:, ch * 2048 + tc * 512: ch * 2048 + (tc + 1) * 512]
                    s.add("tensor", lambda q, ps=pu, a=st_ap, r=rhs, ch=ch:
                          q.matmul(ps, a, r, start=(ch == 0), stop=(ch == 7)),
                          inc=[("pe", 1)], wait=w)
                pe_u = s.cnt["pe"]
                gu_pe[hh % 3] = pe_u
                sgs = s_sg[:, tc * 512:(tc + 1) * 512]
                s.add("scalar", lambda q, o=sgs, i_=pg: q.activation(
                    o, i_, AF.Silu, scale=S127),
                    inc=[("ac", 1)], wait=[("pe", pe_g), ("dv", sg_dv[tc])])
                gate_bank_ac[tb] = s.cnt["ac"]
                ac_sg = s.cnt["ac"]
                hsl = s_hhh[:, hb * 2048 + tc * 512: hb * 2048 + (tc + 1) * 512]
                s.add("vector", lambda q, o=hsl, i_=pu, g_=sgs: q.scalar_tensor_tensor(
                    o, i_, S127, g_, ALU.mult, ALU.mult),
                    inc=[("dv", 1)],
                    wait=[("ac", ac_sg), ("pe", pe_u), ("ds", hhh_ds[hb])])
                up_bank_dv[tb] = s.cnt["dv"]
                sg_dv[tc] = s.cnt["dv"]
                dv_last_stt = s.cnt["dv"]
            if DEBUG and hh == 2:
                s.dma("sync", sg_dbg.ap()[:, :], s_sg[:],
                      wait=[("ac", s.cnt["ac"]), ("dv", dv_last_stt)])
            hhh_ds[hb] = s.dma("sync", h_dram.ap()[hh * 128:(hh + 1) * 128, :],
                               s_hhh[:, hb * 2048:(hb + 1) * 2048],
                               wait=[("dv", dv_last_stt)])
            hh_store_ds = hhh_ds[hb]

        ht_ds = [0, 0]
        ht_pe = [0, 0]
        dps_ac = [0, 0]
        gstg_ds = [0, 0]
        ds_g_done = 0
        for tc in range(4 if STAGES["C"] else 0):
            tb = tc % 2
            ht_ds[tb] = s.dma("sync",
                              s_ht[:, tb * 12288:(tb + 1) * 12288]
                              .rearrange("p (a t) -> p a t", t=512),
                              h_dram.ap()[:, tc * 512:(tc + 1) * 512]
                              .rearrange("(a p) t -> p a t", p=128),
                              wait=[("ds", hh_store_ds), ("pe", ht_pe[tb])])
            for cp in range(8):
                k = (tc * 8 + cp) % 2
                pd = main[:, 2048 + k * 512: 2048 + (k + 1) * 512]
                for a in range(24):
                    w = [("ds", ht_ds[tb]), ("da", da_wd), ("ac", dps_ac[k])] if a == 0 else []
                    st_ap = s_wdb[:, a * 1024 + cp * 128: a * 1024 + (cp + 1) * 128]
                    rhs = s_ht[:, tb * 12288 + a * 512: tb * 12288 + (a + 1) * 512]
                    s.add("tensor", lambda q, ps=pd, a_=st_ap, r=rhs, a2=a:
                          q.matmul(ps, a_, r, start=(a2 == 0), stop=(a2 == 23)),
                          inc=[("pe", 1)], wait=w)
                pe_d = s.cnt["pe"]
                ht_pe[tb] = pe_d
                gst = s_gstg[:, k * 512:(k + 1) * 512]
                s.add("scalar", lambda q, o=gst, i_=pd: q.activation(
                    o, i_, AF.Silu, scale=S127),
                    inc=[("ac", 1)], wait=[("pe", pe_d), ("ds", gstg_ds[k])])
                dps_ac[k] = s.cnt["ac"]
                gstg_ds[k] = s.dma("sync",
                                   g_dram.ap()[cp * 128:(cp + 1) * 128,
                                               tc * 512:(tc + 1) * 512],
                                   gst, wait=[("ac", dps_ac[k])])
        if DEBUG and STAGES["B2"]:
            s.dma("sync", rsout_dbg.ap()[:, :], rs_out.ap()[:, :], wait=[("cc", 1)])
        ds_g_done = s.cnt["ds"]
        pe_C_end = s.cnt["pe"]

        d_dv = [0, 0]
        p_store_ds = 0
        for cp in range(8 if STAGES["D"] else 0):
            for tc in range(4):
                k = (cp * 4 + tc) % 2
                s.dma("sync", s_s2[:, k * 512:(k + 1) * 512],
                      rs_out.ap()[cp * 128:(cp + 1) * 128, tc * 512:(tc + 1) * 512],
                      wait=[("cc", 1), ("dv", d_dv[k]), ("pe", pe_C_end),
                            ("ds", ds_g_done)])
                dsv = s.dma("sync", s_g2[:, k * 512:(k + 1) * 512],
                            g_dram.ap()[cp * 128:(cp + 1) * 128, tc * 512:(tc + 1) * 512],
                            wait=[("ds", ds_g_done)])
                ptl = s_pt[:, k * 512:(k + 1) * 512]
                s.add("vector", lambda q, o=ptl, s_=s_s2[:, k * 512:(k + 1) * 512],
                      g_=s_g2[:, k * 512:(k + 1) * 512], b=s_bproj[:, cp:cp + 1]:
                      q.scalar_tensor_tensor(o, s_, b, g_, ALU.add, ALU.mult),
                      inc=[("dv", 1)], wait=[("ds", dsv)])
                d_dv[k] = s.cnt["dv"]
                p_store_ds = s.dma("sync",
                                   p_dram.ap()[cp * 128:(cp + 1) * 128,
                                               tc * 512:(tc + 1) * 512],
                                   ptl, wait=[("dv", d_dv[k])])
        out_dv = [0, 0]
        for tt in range(16 if STAGES["D"] else 0):
            k = tt % 2
            da_p = s.dmaT("scalar", s_ptT[:, k * 1024:(k + 1) * 1024],
                          p_dram.ap()[:, tt * 128:(tt + 1) * 128],
                          wait=[("ds", p_store_ds), ("dv", out_dv[k])])
            ds_x = s.dma("sync", s_xo[:, k * 1024:(k + 1) * 1024],
                         x_own.ap()[tt * 128:(tt + 1) * 128, :],
                         wait=[("dv", out_dv[k])])
            s.add("vector", lambda q, o=s_outt[:, k * 1024:(k + 1) * 1024],
                  a=s_xo[:, k * 1024:(k + 1) * 1024],
                  b=s_ptT[:, k * 1024:(k + 1) * 1024]:
                  q.tensor_add(o, a, b),
                  inc=[("dv", 1)], wait=[("da", da_p), ("ds", ds_x)])
            out_dv[k] = s.cnt["dv"]
            s.dma("sync", out_own.ap()[tt * 128:(tt + 1) * 128, :],
                  s_outt[:, k * 1024:(k + 1) * 1024],
                  wait=[("dv", out_dv[k])])
        ds_total = s.cnt["ds"]
        s.add("sync", lambda q: q.nop(), wait=[("ds", ds_total), ("da", s.cnt["da"]),
                                               ("dp", s.cnt["dp"])])
        s.emit(block)
    return nc


_NC_CACHE = {}


def kernel(L0_attn_x, L0_attn_W_attn, L0_attn_b_attn, L0_attn_W_proj, L0_attn_b_proj,
           L0_ffn_w_gate, L0_ffn_w_up, L0_ffn_w_down):
    x = np.asarray(L0_attn_x)
    W_attn = np.asarray(L0_attn_W_attn)
    b_attn = np.asarray(L0_attn_b_attn)
    W_proj = np.asarray(L0_attn_W_proj)
    b_proj = np.asarray(L0_attn_b_proj)
    wgT = np.ascontiguousarray(np.asarray(L0_ffn_w_gate).T)
    wuT = np.ascontiguousarray(np.asarray(L0_ffn_w_up).T)
    wdT = np.ascontiguousarray(np.asarray(L0_ffn_w_down).T)

    identc = np.eye(128, dtype=np.float32).astype(ml_dtypes.bfloat16)
    mw = np.zeros((128, 896), dtype=np.float32)
    r = np.arange(128)[:, None]
    cc = np.arange(896)[None, :]
    mw[(r <= cc - 384)] = 1.0
    maskwc = mw.astype(ml_dtypes.bfloat16)
    onesc = np.ones((1, 128), dtype=np.float32)
    bproj_t = np.ascontiguousarray(b_proj.reshape(8, 128).T.astype(np.float32))

    if "nc" not in _NC_CACHE:
        _NC_CACHE["nc"] = build_nc()
    nc = _NC_CACHE["nc"]

    in_maps = []
    for core in range(8):
        b, j = core // 4, core % 4
        qs = W_attn[:, 256 * j:256 * (j + 1)]
        ks = W_attn[:, 1024 + 256 * j:1024 + 256 * (j + 1)]
        vs = W_attn[:, 2048 + 256 * j:2048 + 256 * (j + 1)]
        wqkv = np.ascontiguousarray(np.concatenate([qs, ks, vs], axis=1))
        bq = np.concatenate([b_attn[256 * j:256 * (j + 1)],
                             b_attn[1024 + 256 * j:1024 + 256 * (j + 1)],
                             b_attn[2048 + 256 * j:2048 + 256 * (j + 1)]])
        bqkv_t = np.ascontiguousarray(bq.reshape(6, 128).T.astype(np.float32))
        in_maps.append({
            "x_full": np.ascontiguousarray(x[b]),
            "x_own": np.ascontiguousarray(x[b, j * TS:(j + 1) * TS]),
            "wqkv": wqkv,
            "bqkv": bqkv_t,
            "wproj": np.ascontiguousarray(W_proj[256 * j:256 * (j + 1), :]),
            "bproj": bproj_t,
            "wgT": wgT, "wuT": wuT, "wdT": wdT,
            "ident": identc, "maskw": maskwc, "ones1": onesc,
        })
    res = run_bass_kernel_spmd(nc, in_maps, core_ids=list(range(8)))
    out = np.empty((2, T, C), dtype=np.float32)
    for core in range(8):
        b, j = core // 4, core % 4
        out[b, j * TS:(j + 1) * TS] = res.results[core]["out_own"]
    return out


# version 03
# speedup vs baseline: 1.7507x; 1.7507x over previous
import numpy as np
import ml_dtypes
from contextlib import ExitStack

import concourse.bass as bass
import concourse.mybir as mybir
from concourse.bass_utils import run_bass_kernel_spmd

F32 = mybir.dt.float32
BF16 = mybir.dt.bfloat16
I32 = mybir.dt.int32
AF = mybir.ActivationFunctionType
ALU = mybir.AluOpType
AX = mybir.AxisListType

T = 8192
C = 1024
D = 64
TS = 2048
HID = 3072
QC = 512
NQC = T // QC
NKT = T // 128
S127 = float(np.float32(1.0 / 127.0))
SCALE = 0.125
DEBUG = False
STAGES = {"QKV": True, "B": True, "B2": True, "C": True, "D": True}


class Sched:
    ENGINES = ("sync", "tensor", "vector", "scalar", "gpsimd")

    def __init__(self, nc, ctx):
        self.nc = nc
        self.q = {e: [] for e in self.ENGINES}
        self.sems = {}
        self.cnt = {}
        for name in ("ds", "da", "dp", "pe", "dv", "ac", "cc"):
            self.sems[name] = ctx.enter_context(nc.semaphore(name))
            self.cnt[name] = 0

    def add(self, eng, fn, inc=(), wait=()):
        w = [(n, v) for (n, v) in wait if v > 0]
        for n, v in w:
            assert v <= self.cnt[n], f"wait on future: {n} {v} > {self.cnt[n]}"
        self.q[eng].append((w, fn, list(inc)))
        for n, v in inc:
            self.cnt[n] += v

    def dma(self, eng, out, in_, wait=()):
        sem = {"sync": "ds", "scalar": "da", "gpsimd": "dp"}[eng]
        self.add(eng, lambda q, out=out, in_=in_: q.dma_start(out=out, in_=in_),
                 inc=[(sem, 16)], wait=wait)
        return self.cnt[sem]

    def dmaT(self, eng, out, in_, wait=()):
        sem = {"sync": "ds", "scalar": "da"}[eng]
        self.add(eng, lambda q, out=out, in_=in_: q.dma_start_transpose(out, in_),
                 inc=[(sem, 16)], wait=wait)
        return self.cnt[sem]

    def emit(self, block):
        for eng in self.ENGINES:
            ops = self.q[eng]
            sems = self.sems

            def mk(ops):
                def go(q):
                    hi = {}
                    for w, fn, i in ops:
                        for n, v in w:
                            if v > hi.get(n, 0):
                                q.wait_ge(sems[n], v)
                                hi[n] = v
                        instr = fn(q)
                        for n, v in i:
                            instr.then_inc(sems[n], v)
                return go
            getattr(block, eng)(mk(ops))


def build_nc():
    nc = bass.Bass("TRN2", target_bir_lowering=False)

    x_full = nc.dram_tensor("x_full", [T, C], F32, kind="ExternalInput")
    x_own = nc.dram_tensor("x_own", [TS, C], F32, kind="ExternalInput")
    wqkv = nc.dram_tensor("wqkv", [C, 768], F32, kind="ExternalInput")
    bqkv = nc.dram_tensor("bqkv", [128, 6], F32, kind="ExternalInput")
    wproj = nc.dram_tensor("wproj", [256, C], F32, kind="ExternalInput")
    bproj = nc.dram_tensor("bproj", [128, 8], F32, kind="ExternalInput")
    wgT = nc.dram_tensor("wgT", [C, HID], I32, kind="ExternalInput")
    wuT = nc.dram_tensor("wuT", [C, HID], I32, kind="ExternalInput")
    wdT = nc.dram_tensor("wdT", [HID, C], I32, kind="ExternalInput")
    ident = nc.dram_tensor("ident", [128, 128], BF16, kind="ExternalInput")
    maskw = nc.dram_tensor("maskw", [128, 896], BF16, kind="ExternalInput")
    ones1 = nc.dram_tensor("ones1", [1, 128], F32, kind="ExternalInput")
    out_own = nc.dram_tensor("out_own", [TS, C], F32, kind="ExternalOutput")

    dbg = "ExternalOutput" if DEBUG else "Internal"
    xn_dram = nc.dram_tensor("xn_dram", [T, C], BF16, kind=dbg)
    xn_own_dram = nc.dram_tensor("xn_own_dram", [TS, C], BF16, kind="Internal")
    h_dram = nc.dram_tensor("h_dram", [HID, TS], BF16, kind=dbg)
    p_dram = nc.dram_tensor("p_dram", [C, TS], BF16, kind="Internal")
    wg_bf = nc.dram_tensor("wg_bf", [C, HID], BF16, kind="Internal")
    wu_bf = nc.dram_tensor("wu_bf", [C, HID], BF16, kind="Internal")
    wd_bf = nc.dram_tensor("wd_bf", [HID, C], BF16, kind="Internal")
    rs_in = nc.dram_tensor("rs_in", [4 * C, TS], F32, kind="Internal")
    rs_out = nc.dram_tensor("rs_out", [C, TS], F32, kind="Internal")
    if DEBUG:
        qt_dbg = nc.dram_tensor("qt_dbg", [128, 16384], BF16, kind="ExternalOutput")
        kt_dbg = nc.dram_tensor("kt_dbg", [128, 16384], BF16, kind="ExternalOutput")
        vt_dbg = nc.dram_tensor("vt_dbg", [128, 16640], BF16, kind="ExternalOutput")
        y_dbg = nc.dram_tensor("y_dbg", [128, 16384], BF16, kind="ExternalOutput")
        rsin_dbg = nc.dram_tensor("rsin_dbg", [4 * C, TS], F32, kind="ExternalOutput")
        wgu_dbg = nc.dram_tensor("wgu_dbg", [128, 6144], BF16, kind="ExternalOutput")
        rsout_dbg = nc.dram_tensor("rsout_dbg", [C, TS], F32, kind="ExternalOutput")

    BASE = 24576
    TOP = 229344
    allocs = []

    def sb(name, cols, dtype, off, ph0, ph1):
        size = cols * (2 if dtype in (BF16,) else 4)
        assert off % 32 == 0 and off >= BASE and off + size <= TOP, (name, off, size)
        for (n2, o2, s2, p0, p1) in allocs:
            if not (ph1 < p0 or p1 < ph0):
                assert off + size <= o2 or o2 + s2 <= off, f"overlap {name} vs {n2}"
        allocs.append((name, off, size, ph0, ph1))
        return nc.alloc_sbuf_tensor_at(name, [128, cols], dtype, offset=off)

    s_ident = sb("s_ident", 128, BF16, 24576, 0, 4)
    s_maskw = sb("s_maskw", 896, BF16, 24832, 0, 4)
    s_ones = nc.alloc_sbuf_tensor_at("s_ones", [1, 128], F32, offset=26624)
    allocs.append(("s_ones", 26624, 512, 0, 4))
    s_bqkv = sb("s_bqkv", 8, F32, 27136, 0, 4)
    s_bproj = sb("s_bproj", 8, F32, 27168, 0, 4)
    s_stats = sb("s_stats", 48, F32, 27200, 0, 4)
    s_xnt_own = sb("s_xnt_own", 16384, BF16, 27392, 0, 4)
    s_wprojb = sb("s_wprojb", 2048, BF16, 60160, 0, 4)

    s_QT = sb("s_QT", 16384, BF16, 64256, 0, 2)
    s_KT = sb("s_KT", 16384, BF16, 97024, 0, 2)
    s_VT = sb("s_VT", 16640, BF16, 129792, 0, 2)
    s_ySB = sb("s_ySB", 16384, BF16, 163072, 1, 2)

    s_xbuf = sb("s_xbuf", 4096, F32, 163072, 0, 0)
    s_sqbuf = sb("s_sqbuf", 1024, F32, 179456, 0, 0)
    s_xnst = sb("s_xnst", 4096, BF16, 220416, 0, 0)
    s_xntb = sb("s_xntb", 8192, BF16, 187648, 0, 0)
    s_wqkvb = sb("s_wqkvb", 6144, BF16, 204032, 0, 0)
    s_vstage = sb("s_vstage", 2048, BF16, 216320, 0, 0)

    s_ptb = sb("s_ptb", 4096, BF16, 195840, 1, 2)
    s_deninv = sb("s_deninv", 1024, F32, 204032, 1, 2)
    s_ycp = sb("s_ycp", 1024, F32, 208128, 1, 2)
    s_spstg = sb("s_spstg", 2048, F32, 212224, 1, 2)

    s_wdb = sb("s_wdb", 24576, BF16, 64256, 3, 3)
    s_wgu = sb("s_wgu", 6144, BF16, 113408, 3, 3)
    s_sg = sb("s_sg", 2048, F32, 125696, 3, 3)
    s_hhh = sb("s_hhh", 4096, BF16, 133888, 3, 3)
    s_ht = sb("s_ht", 24576, BF16, 142080, 3, 3)
    s_gstg = sb("s_gstg", 1024, F32, 191232, 3, 3)

    s_s2 = sb("s_s2", 1024, F32, 195328, 3, 4)
    s_pt = sb("s_pt", 1024, BF16, 199424, 3, 4)
    s_ptT = sb("s_ptT", 2048, BF16, 201472, 3, 4)
    s_xo = sb("s_xo", 2048, F32, 205568, 3, 4)
    s_outt = sb("s_outt", 2048, F32, 213760, 3, 4)

    with ExitStack() as ctx:
        main = ctx.enter_context(nc.psum_tensor("main", [128, 3072], F32))
        ptA = ctx.enter_context(nc.psum_tensor("ptA", [128, 1024], BF16))
        ptB = ctx.enter_context(nc.psum_tensor("ptB", [128, 1024], BF16))
        s = Sched(nc, ctx)
        block = ctx.enter_context(nc.Block())

        s.dma("sync", s_ident[:], ident[:, :])
        s.dma("sync", s_maskw[:], maskw[:, :])
        s.dma("sync", s_ones[:], ones1[:, :])
        s.dma("sync", s_bqkv[:, 0:6], bqkv[:, :])
        ds_consts = s.dma("sync", s_bproj[:, 0:8], bproj[:, :])
        s.dma("gpsimd", s_wqkvb[:].rearrange("p (a d) -> p a d", d=768),
              wqkv.ap().rearrange("(a p) d -> p a d", p=128))
        dp_wqkv = s.cnt["dp"]
        s.dma("gpsimd", s_wprojb[:].rearrange("p (a d) -> p a d", d=1024),
              wproj.ap().rearrange("(a p) d -> p a d", p=128))
        dp_wproj = s.cnt["dp"]
        for half in range(2):
            s.dma("gpsimd", wg_bf.ap()[half * 512:(half + 1) * 512, :],
                  wgT.ap()[half * 512:(half + 1) * 512, :])
            s.dma("gpsimd", wu_bf.ap()[half * 512:(half + 1) * 512, :],
                  wuT.ap()[half * 512:(half + 1) * 512, :])
            s.dma("gpsimd", wd_bf.ap()[half * 1536:(half + 1) * 1536, :],
                  wdT.ap()[half * 1536:(half + 1) * 1536, :])
        dp_wbf = s.cnt["dp"]
        s.add("vector", lambda q: q.memset(s_VT[:], 1.0), inc=[("dv", 1)])
        dv_vtones = s.cnt["dv"]

        xcons_dv = [0, 0, 0, 0]
        xcons_ac = [0, 0, 0, 0]
        xnst_ds = [0, 0, 0, 0]
        stats_free_ac = [0, 0, 0, 0]
        x_loaded_ds = [0, 0, 0, 0]

        stats_free_dv = [0, 0, 0, 0]

        def ln_stage1(ti, src_rows):
            slot = ti % 4
            sb_ = slot * 10
            st = s_stats
            x_loaded_ds[slot] = s.dma(
                "sync", s_xbuf[:, slot * 1024:(slot + 1) * 1024], src_rows,
                wait=[("dv", xcons_dv[slot]), ("ac", xcons_ac[slot])])
            xs = s_xbuf[:, slot * 1024:(slot + 1) * 1024]
            s.add("vector", lambda q, xs=xs, o=st[:, sb_:sb_ + 1]: q.reduce_sum(o, xs, axis=AX.X),
                  inc=[("dv", 1)],
                  wait=[("ds", x_loaded_ds[slot]), ("ac", stats_free_ac[slot]),
                        ("dv", stats_free_dv[slot])])
            s.add("scalar", lambda q, xs=xs, o=st[:, sb_ + 1:sb_ + 2]: q.activation(
                s_sqbuf[:], xs, AF.Square, accum_out=o),
                inc=[("ac", 1)],
                wait=[("ds", x_loaded_ds[slot]), ("dv", stats_free_dv[slot])])
            return s.cnt["dv"], s.cnt["ac"]

        def ln_stage2(ti, marks, dst_dram, dst_row):
            slot = ti % 4
            sb_ = slot * 10
            st = s_stats
            dv_red, ac_sq = marks
            a_s12 = st[:, sb_ + 0:sb_ + 2]
            a_mex = st[:, sb_ + 2:sb_ + 4]
            a_m = st[:, sb_ + 2:sb_ + 3]
            a_ex2 = st[:, sb_ + 3:sb_ + 4]
            a_nvar = st[:, sb_ + 5:sb_ + 6]
            a_sd = st[:, sb_ + 6:sb_ + 7]
            a_rstd = st[:, sb_ + 7:sb_ + 8]
            a_negb = st[:, sb_ + 8:sb_ + 9]
            xs = s_xbuf[:, slot * 1024:(slot + 1) * 1024]
            s.add("vector", lambda q, o=a_mex, i_=a_s12: q.tensor_scalar_mul(
                o, i_, 1.0 / 1024.0),
                inc=[("dv", 1)], wait=[("ac", ac_sq), ("dv", dv_red)])
            dv_mex = s.cnt["dv"]
            s.add("vector", lambda q, o=a_nvar, i_=a_m, e_=a_ex2: q.scalar_tensor_tensor(
                o, i_, i_, e_, ALU.mult, ALU.subtract),
                inc=[("dv", 1)], wait=[("dv", dv_mex)])
            dv_var = s.cnt["dv"]
            s.add("scalar", lambda q, o=a_sd, i_=a_nvar: q.activation(
                o, i_, AF.Sqrt, scale=-1.0, bias=np.float32(1e-5)),
                inc=[("ac", 1)], wait=[("dv", dv_var)])
            ac_sd = s.cnt["ac"]
            s.add("vector", lambda q, o=a_rstd, i_=a_sd: q.reciprocal(o, i_),
                  inc=[("dv", 1)], wait=[("ac", ac_sd)])
            dv_rstd = s.cnt["dv"]
            s.add("vector", lambda q, o=a_negb, i_=a_m, r_=a_rstd: q.scalar_tensor_tensor(
                o, i_, -1.0, r_, ALU.mult, ALU.mult),
                inc=[("dv", 1)], wait=[("dv", dv_rstd)])
            dv_nb = s.cnt["dv"]
            xcons_dv[slot] = dv_nb
            stats_free_dv[slot] = dv_nb
            xo = s_xnst[:, slot * 1024:(slot + 1) * 1024]
            s.add("scalar", lambda q, xs=xs, xo=xo, sc_=a_rstd, b_=a_negb: q.activation(
                xo, xs, AF.Identity, scale=sc_, bias=b_),
                inc=[("ac", 1)], wait=[("dv", dv_nb), ("ds", xnst_ds[slot])])
            xcons_ac[slot] = s.cnt["ac"]
            stats_free_ac[slot] = s.cnt["ac"]
            xnst_ds[slot] = s.dma("sync", dst_dram[dst_row:dst_row + 128, :], xo,
                                  wait=[("ac", s.cnt["ac"])])
            return xnst_ds[slot]

        def ln_group(tis, srcs, dst_dram, rows):
            marks = [ln_stage1(ti, sr) for ti, sr in zip(tis, srcs)]
            last = 0
            for ti, mk, row in zip(tis, marks, rows):
                last = ln_stage2(ti, mk, dst_dram, row)
            return last

        xntb_pe = [0, 0]
        qkvps_dv = [0, 0]
        vstage_pe = 0
        ptps_dv = [dv_vtones, dv_vtones]
        n_qkv = [0]

        def qkv_chunk3(sc, ds_ready):
            nonlocal vstage_pe
            da = 0
            for ch in range(8):
                da = s.dmaT("scalar",
                            s_xntb[:, ch * 1024:(ch + 1) * 1024],
                            xn_dram.ap()[sc * 1024:(sc + 1) * 1024,
                                         ch * 128:(ch + 1) * 128],
                            wait=[("ds", ds_ready), ("pe", xntb_pe[0])])
            for dp_ in range(6):
                for tcc in range(2):
                    k = n_qkv[0] % 2
                    n_qkv[0] += 1
                    ps = main[:, k * 512:(k + 1) * 512]
                    for ch in range(8):
                        w = [("da", da), ("dp", dp_wqkv), ("dv", qkvps_dv[k])] if ch == 0 else []
                        st_ap = s_wqkvb[:, ch * 768 + dp_ * 128: ch * 768 + (dp_ + 1) * 128]
                        rhs = s_xntb[:, ch * 1024 + tcc * 512: ch * 1024 + tcc * 512 + 512]
                        s.add("tensor", lambda q, ps=ps, st_ap=st_ap, rhs=rhs, ch=ch:
                              q.matmul(ps, st_ap, rhs, start=(ch == 0), stop=(ch == 7)),
                              inc=[("pe", 1)], wait=w)
                    pe_mm = s.cnt["pe"]
                    xntb_pe[0] = pe_mm
                    t0 = sc * 1024 + tcc * 512
                    if dp_ < 2:
                        dst = s_QT[:, dp_ * 8192 + t0: dp_ * 8192 + t0 + 512]
                    elif dp_ < 4:
                        dst = s_KT[:, (dp_ - 2) * 8192 + t0: (dp_ - 2) * 8192 + t0 + 512]
                    else:
                        dst = s_vstage[:, (dp_ - 4) * 1024 + tcc * 512:
                                       (dp_ - 4) * 1024 + tcc * 512 + 512]
                    bias_ap = s_bqkv[:, dp_:dp_ + 1]
                    vw = [("pe", pe_mm)]
                    if dp_ >= 4:
                        vw.append(("pe", vstage_pe))
                    s.add("vector", lambda q, dst=dst, ps=ps, b=bias_ap:
                          q.tensor_scalar(dst, ps, b, None, ALU.add),
                          inc=[("dv", 1)], wait=vw)
                    qkvps_dv[k] = s.cnt["dv"]
                    if dp_ >= 4:
                        pair = dp_ - 4
                        dv_evac = s.cnt["dv"]
                        for i in range(4):
                            kt = sc * 8 + tcc * 4 + i
                            pt_ps = (ptA if (kt % 2 == 0) else ptB)[:, 0:128]
                            vsl = s_vstage[:, pair * 1024 + tcc * 512 + i * 128:
                                           pair * 1024 + tcc * 512 + (i + 1) * 128]
                            s.add("tensor", lambda q, o=pt_ps, i_=vsl:
                                  q.transpose(o, i_, s_ident[:]),
                                  inc=[("pe", 1)],
                                  wait=[("dv", dv_evac), ("dv", ptps_dv[kt % 2]),
                                        ("ds", ds_consts)])
                            pe_tr = s.cnt["pe"]
                            base = kt * 260 + 130 * pair
                            dst2 = s_VT[:, base:base + 130].rearrange(
                                "p (h x) -> p h x", x=65)[:, 0:2, 0:64]
                            src2 = pt_ps.rearrange("p (h x) -> p h x", x=64)
                            s.add("vector", lambda q, d=dst2, s2=src2: q.tensor_copy(d, s2),
                                  inc=[("dv", 1)], wait=[("pe", pe_tr)])
                            ptps_dv[kt % 2] = s.cnt["dv"]
                        vstage_pe = s.cnt["pe"]

        for sc in range(8):
            ds_last = 0
            for g in range(2):
                tis = [sc * 8 + g * 4 + t for t in range(4)]
                ds_last = ln_group(
                    tis,
                    [x_full.ap()[ti * 128:(ti + 1) * 128, :] for ti in tis],
                    xn_dram, [ti * 128 for ti in tis])
            if STAGES["QKV"]:
                qkv_chunk3(sc, ds_last)

        ds_own = 0
        for g in range(4):
            tts = [g * 4 + t for t in range(4)]
            ds_own = ln_group(
                [64 + tt for tt in tts],
                [x_own.ap()[tt * 128:(tt + 1) * 128, :] for tt in tts],
                xn_own_dram, [tt * 128 for tt in tts])
        for ch in range(8):
            s.dmaT("scalar", s_xnt_own[:, ch * 2048:(ch + 1) * 2048],
                   xn_own_dram.ap()[:, ch * 128:(ch + 1) * 128],
                   wait=[("ds", ds_own)])
        da_xnt_own = s.cnt["da"]

        dv_A_end = s.cnt["dv"]
        if DEBUG:
            s.dma("sync", qt_dbg.ap()[:, :], s_QT[:], wait=[("dv", dv_A_end)])
            s.dma("sync", kt_dbg.ap()[:, :], s_KT[:], wait=[("dv", dv_A_end)])
            s.dma("sync", vt_dbg.ap()[:, :], s_VT[:], wait=[("dv", dv_A_end)])
        last_exp_ac = [0, 0]
        last_av_pe = [0, 0, 0, 0]
        last_bc_dv = dv_A_end
        last_av_cons_dv = dv_A_end

        stb = [main[:, 0:1024], main[:, 1024:2048]]
        avps = main[:, 2048:3072]

        def emit_qk(qc, pair, kt):
            b_ = kt % 2
            w = [("ac", last_exp_ac[b_]), ("dv", dv_A_end)]
            if kt <= 1:
                w.append(("dv", last_bc_dv))
            stA = s_KT[0:64, pair * 8192 + kt * 128: pair * 8192 + (kt + 1) * 128]
            stB = s_KT[64:128, pair * 8192 + kt * 128: pair * 8192 + (kt + 1) * 128]
            rhA = s_QT[0:64, pair * 8192 + qc * 512: pair * 8192 + (qc + 1) * 512]
            rhB = s_QT[64:128, pair * 8192 + qc * 512: pair * 8192 + (qc + 1) * 512]
            s.add("tensor", lambda q, o=stb[b_][:, 0:512], a=stA, r=rhA:
                  q.matmul(o, a, r, start=True, stop=True),
                  inc=[("pe", 1)], wait=w)
            s.add("tensor", lambda q, o=stb[b_][:, 512:1024], a=stB, r=rhB:
                  q.matmul(o, a, r, start=True, stop=True, tile_position=(64, 0)),
                  inc=[("pe", 1)])
            return s.cnt["pe"]

        def emit_expav(qc, pair, kt, pe_qk, K):
            b_ = kt % 2
            p4 = kt % 4
            pt = s_ptb[:, p4 * 1024:(p4 + 1) * 1024]
            s.add("scalar", lambda q, o=pt, i_=stb[b_][:, 0:1024]:
                  q.activation(o, i_, AF.Exp, scale=SCALE),
                  inc=[("ac", 1)],
                  wait=[("pe", max(pe_qk, last_av_pe[p4]))])
            last_exp_ac[b_] = s.cnt["ac"]
            ac_exp = s.cnt["ac"]
            avw = [("ac", ac_exp)]
            if kt >= 4 * qc:
                o = kt * 128 - qc * 512
                msk = s_maskw[:, 384 - o: 896 - o]
                s.add("vector", lambda q, p0=pt[:, 0:512], m=msk:
                      q.tensor_mul(p0, p0, m), inc=[("dv", 1)],
                      wait=[("ac", ac_exp)])
                s.add("vector", lambda q, p1=pt[:, 512:1024], m=msk:
                      q.tensor_mul(p1, p1, m), inc=[("dv", 1)])
                avw.append(("dv", s.cnt["dv"]))
            if kt == 0:
                avw.append(("dv", last_av_cons_dv))
            vA = s_VT[:, kt * 260 + 130 * pair: kt * 260 + 130 * pair + 65]
            vB = s_VT[:, kt * 260 + 130 * pair + 65: kt * 260 + 130 * pair + 130]
            s.add("tensor", lambda q, v=vA, p=pt[:, 0:512], st_=(kt == 0), sp=(kt == K - 1):
                  q.matmul(avps[0:65, 0:512], v, p, start=st_, stop=sp),
                  inc=[("pe", 1)], wait=avw)
            s.add("tensor", lambda q, v=vB, p=pt[:, 512:1024], st_=(kt == 0), sp=(kt == K - 1):
                  q.matmul(avps[0:65, 512:1024], v, p, start=st_, stop=sp),
                  inc=[("pe", 1)])
            last_av_pe[p4] = s.cnt["pe"]

        for qc in range(NQC if STAGES["B"] else 0):
            for pair in range(2):
                K = 4 * qc + 4
                pe_qks = {}
                for kt in range(K):
                    pe_qks[kt] = emit_qk(qc, pair, kt)
                    if kt >= 1:
                        emit_expav(qc, pair, kt - 1, pe_qks[kt - 1], K)
                emit_expav(qc, pair, K - 1, pe_qks[K - 1], K)
                pe_av_done = s.cnt["pe"]
                s.add("vector", lambda q: q.reciprocal(
                    s_deninv[0:1, 0:512], avps[64:65, 0:512]),
                    inc=[("dv", 1)], wait=[("pe", pe_av_done)])
                s.add("vector", lambda q: q.reciprocal(
                    s_deninv[0:1, 512:1024], avps[64:65, 512:1024]),
                    inc=[("dv", 1)])
                dv_rec = s.cnt["dv"]
                s.add("vector", lambda q: q.tensor_copy(
                    s_ycp[0:64, 0:512], avps[0:64, 0:512]), inc=[("dv", 1)])
                s.add("vector", lambda q: q.tensor_copy(
                    s_ycp[64:128, 0:512], avps[0:64, 512:1024]), inc=[("dv", 1)])
                last_av_cons_dv = s.cnt["dv"]
                dv_ycp = s.cnt["dv"]
                s.add("tensor", lambda q: q.matmul(
                    stb[0][:, 0:512], s_ones[0:1, :], s_deninv[0:1, 0:512],
                    start=True, stop=True),
                    inc=[("pe", 1)],
                    wait=[("dv", dv_rec), ("ac", s.cnt["ac"])])
                s.add("tensor", lambda q: q.matmul(
                    stb[0][:, 512:1024], s_ones[0:1, :], s_deninv[0:1, 512:1024],
                    start=True, stop=True), inc=[("pe", 1)])
                pe_bc = s.cnt["pe"]
                yslA = s_ySB[0:64, pair * 8192 + qc * 512: pair * 8192 + (qc + 1) * 512]
                yslB = s_ySB[64:128, pair * 8192 + qc * 512: pair * 8192 + (qc + 1) * 512]
                s.add("vector", lambda q, o=yslA, i0=s_ycp[0:64, 0:512],
                      i1=stb[0][0:64, 0:512]: q.tensor_mul(o, i0, i1),
                      inc=[("dv", 1)], wait=[("pe", pe_bc), ("dv", dv_ycp)])
                s.add("vector", lambda q, o=yslB, i0=s_ycp[64:128, 0:512],
                      i1=stb[0][64:128, 512:1024]: q.tensor_mul(o, i0, i1),
                      inc=[("dv", 1)])
                last_bc_dv = s.cnt["dv"]

        dv_B_end = s.cnt["dv"]
        pe_B_end = s.cnt["pe"]
        b2_dv = [0, 0, 0, 0]
        b2_ds = [0, 0, 0, 0]
        for tcq in range(NQC if STAGES["B2"] else 0):
            for cp in range(8):
                k = (tcq * 8 + cp) % 4
                ps = main[:, k * 512:(k + 1) * 512]
                for pr in range(2):
                    w = [("dv", max(b2_dv[k], dv_B_end)), ("dp", dp_wproj)] if pr == 0 else []
                    st_ap = s_wprojb[:, pr * 1024 + cp * 128: pr * 1024 + (cp + 1) * 128]
                    rhs = s_ySB[:, pr * 8192 + tcq * 512: pr * 8192 + (tcq + 1) * 512]
                    s.add("tensor", lambda q, ps=ps, a=st_ap, r=rhs, pr=pr:
                          q.matmul(ps, a, r, start=(pr == 0), stop=(pr == 1)),
                          inc=[("pe", 1)], wait=w)
                pe_mm = s.cnt["pe"]
                stg = s_spstg[:, (k % 2) * 512:(k % 2 + 1) * 512]
                s.add("vector", lambda q, o=stg, i_=ps: q.tensor_copy(o, i_),
                      inc=[("dv", 1)], wait=[("pe", pe_mm), ("ds", b2_ds[k % 2])])
                b2_dv[k] = s.cnt["dv"]
                rb = 1024 * (tcq // 4) + cp * 128
                cb = (tcq % 4) * 512
                b2_ds[k % 2] = s.dma("sync", rs_in.ap()[rb:rb + 128, cb:cb + 512], stg,
                                     wait=[("dv", b2_dv[k])])
        if DEBUG:
            s.dma("sync", y_dbg.ap()[:, :], s_ySB[:], wait=[("dv", s.cnt["dv"])])
            s.dma("sync", rsin_dbg.ap()[:, :], rs_in.ap()[:, :],
                  wait=[("ds", s.cnt["ds"])])
        ds_rs_in = s.cnt["ds"]
        dv_b2_end = s.cnt["dv"]

        wgu_da = []
        for hh in range(2 if STAGES["C"] else 0):
            for wt, wsrc in ((0, wg_bf), (1, wu_bf)):
                s.dma("scalar",
                      s_wgu[:, (hh % 3) * 2048 + wt * 1024:
                            (hh % 3) * 2048 + wt * 1024 + 1024]
                      .rearrange("p (a h) -> p a h", h=128),
                      wsrc.ap()[:, hh * 128:(hh + 1) * 128]
                      .rearrange("(a p) h -> p a h", p=128),
                      wait=[("pe", pe_B_end), ("dp", dp_wbf)])
            wgu_da.append(s.cnt["da"])
        if STAGES["B2"]:
            s.add("gpsimd", lambda q: q.collective_compute(
            "ReduceScatter", ALU.add,
                replica_groups=[[0, 1, 2, 3], [4, 5, 6, 7]],
                ins=[rs_in.ap().opt()], outs=[rs_out.ap().opt()]),
                inc=[("cc", 1)], wait=[("ds", ds_rs_in)])
        for part in range(4 if STAGES["C"] else 0):
            s.dma("scalar",
                  s_wdb[:].rearrange("p (a c) -> p a c", c=1024)[:, part * 6:(part + 1) * 6, :],
                  wd_bf.ap().rearrange("(a p) c -> p a c", p=128)[:, part * 6:(part + 1) * 6, :],
                  wait=[("pe", pe_B_end), ("dp", dp_wbf)])
        da_wd = s.cnt["da"]

        gate_bank_ac = [0, 0]
        up_bank_dv = [0, 0]
        sg_dv = [0, 0, 0, 0]
        hhh_ds = [0, 0]
        gu_pe = [0, 0, 0]
        hh_store_ds = 0
        for hh in range(24 if STAGES["C"] else 0):
            if hh >= 2:
                for wt, wsrc in ((0, wg_bf), (1, wu_bf)):
                    s.dma("scalar",
                          s_wgu[:, (hh % 3) * 2048 + wt * 1024:
                                (hh % 3) * 2048 + wt * 1024 + 1024]
                          .rearrange("p (a h) -> p a h", h=128),
                          wsrc.ap()[:, hh * 128:(hh + 1) * 128]
                          .rearrange("(a p) h -> p a h", p=128),
                          wait=[("pe", gu_pe[hh % 3])])
                wgu_da.append(s.cnt["da"])
            if DEBUG and hh == 2:
                s.dma("sync", wgu_dbg.ap()[:, :], s_wgu[:],
                      wait=[("da", wgu_da[2]), ("dv", dv_b2_end)])
            hb = hh % 2
            dv_last_stt = 0
            for tc in range(4):
                tb = tc % 2
                pg = main[:, tb * 512:(tb + 1) * 512]
                pu = main[:, 1024 + tb * 512: 1024 + (tb + 1) * 512]
                for ch in range(8):
                    w = [("da", max(wgu_da[hh], da_xnt_own)), ("ac", gate_bank_ac[tb]),
                         ("dv", dv_b2_end)] if ch == 0 else []
                    st_ap = s_wgu[:, (hh % 3) * 2048 + ch * 128:(hh % 3) * 2048 + (ch + 1) * 128]
                    rhs = s_xnt_own[:, ch * 2048 + tc * 512: ch * 2048 + (tc + 1) * 512]
                    s.add("tensor", lambda q, ps=pg, a=st_ap, r=rhs, ch=ch:
                          q.matmul(ps, a, r, start=(ch == 0), stop=(ch == 7)),
                          inc=[("pe", 1)], wait=w)
                pe_g = s.cnt["pe"]
                for ch in range(8):
                    w = [("dv", up_bank_dv[tb])] if ch == 0 else []
                    st_ap = s_wgu[:, (hh % 3) * 2048 + 1024 + ch * 128:
                                  (hh % 3) * 2048 + 1024 + (ch + 1) * 128]
                    rhs = s_xnt_own[:, ch * 2048 + tc * 512: ch * 2048 + (tc + 1) * 512]
                    s.add("tensor", lambda q, ps=pu, a=st_ap, r=rhs, ch=ch:
                          q.matmul(ps, a, r, start=(ch == 0), stop=(ch == 7)),
                          inc=[("pe", 1)], wait=w)
                pe_u = s.cnt["pe"]
                gu_pe[hh % 3] = pe_u
                sgs = s_sg[:, tc * 512:(tc + 1) * 512]
                s.add("scalar", lambda q, o=sgs, i_=pg: q.activation(
                    o, i_, AF.Silu, scale=S127),
                    inc=[("ac", 1)], wait=[("pe", pe_g), ("dv", sg_dv[tc])])
                gate_bank_ac[tb] = s.cnt["ac"]
                ac_sg = s.cnt["ac"]
                hsl = s_hhh[:, hb * 2048 + tc * 512: hb * 2048 + (tc + 1) * 512]
                s.add("vector", lambda q, o=hsl, i_=pu, g_=sgs: q.scalar_tensor_tensor(
                    o, i_, S127, g_, ALU.mult, ALU.mult),
                    inc=[("dv", 1)],
                    wait=[("ac", ac_sg), ("pe", pe_u), ("ds", hhh_ds[hb])])
                up_bank_dv[tb] = s.cnt["dv"]
                sg_dv[tc] = s.cnt["dv"]
                dv_last_stt = s.cnt["dv"]
            hhh_ds[hb] = s.dma("sync", h_dram.ap()[hh * 128:(hh + 1) * 128, :],
                               s_hhh[:, hb * 2048:(hb + 1) * 2048],
                               wait=[("dv", dv_last_stt)])
            hh_store_ds = hhh_ds[hb]

        ht_ds = [0, 0]
        ht_pe = [0, 0]
        dps_ac = [0, 0]
        gstg_dv = [0, 0]
        s2_ds = [0, 0]
        s2_dv = [0, 0]
        p_dv = [0, 0]
        out_dv = [0, 0]
        for tc in range(4 if STAGES["C"] else 0):
            tb = tc % 2
            ht_ds[tb] = s.dma("sync",
                              s_ht[:, tb * 12288:(tb + 1) * 12288]
                              .rearrange("p (a t) -> p a t", t=512),
                              h_dram.ap()[:, tc * 512:(tc + 1) * 512]
                              .rearrange("(a p) t -> p a t", p=128),
                              wait=[("ds", hh_store_ds), ("pe", ht_pe[tb])])
            tc_p_ds = 0
            for cp in range(8):
                k = (tc * 8 + cp) % 2
                s2_ds[k] = s.dma("sync", s_s2[:, k * 512:(k + 1) * 512],
                                 rs_out.ap()[cp * 128:(cp + 1) * 128,
                                             tc * 512:(tc + 1) * 512],
                                 wait=[("cc", 1), ("dv", s2_dv[k])])
                pd = main[:, 2048 + k * 512: 2048 + (k + 1) * 512]
                for a in range(24):
                    w = [("ds", ht_ds[tb]), ("da", da_wd), ("ac", dps_ac[k])] if a == 0 else []
                    st_ap = s_wdb[:, a * 1024 + cp * 128: a * 1024 + (cp + 1) * 128]
                    rhs = s_ht[:, tb * 12288 + a * 512: tb * 12288 + (a + 1) * 512]
                    s.add("tensor", lambda q, ps=pd, a_=st_ap, r=rhs, a2=a:
                          q.matmul(ps, a_, r, start=(a2 == 0), stop=(a2 == 23)),
                          inc=[("pe", 1)], wait=w)
                pe_d = s.cnt["pe"]
                ht_pe[tb] = pe_d
                gst = s_gstg[:, k * 512:(k + 1) * 512]
                s.add("scalar", lambda q, o=gst, i_=pd: q.activation(
                    o, i_, AF.Silu, scale=S127),
                    inc=[("ac", 1)], wait=[("pe", pe_d), ("dv", gstg_dv[k])])
                dps_ac[k] = s.cnt["ac"]
                ptl = s_pt[:, k * 512:(k + 1) * 512]
                s.add("vector", lambda q, o=ptl, s_=s_s2[:, k * 512:(k + 1) * 512],
                      g_=gst, b=s_bproj[:, cp:cp + 1]:
                      q.scalar_tensor_tensor(o, s_, b, g_, ALU.add, ALU.mult),
                      inc=[("dv", 1)],
                      wait=[("ac", dps_ac[k]), ("ds", s2_ds[k]), ("ds", p_dv[k])])
                gstg_dv[k] = s.cnt["dv"]
                s2_dv[k] = s.cnt["dv"]
                dv_p = s.cnt["dv"]
                p_dv[k] = s.dma("sync", p_dram.ap()[cp * 128:(cp + 1) * 128,
                                                    tc * 512:(tc + 1) * 512],
                                ptl, wait=[("dv", dv_p)])
                tc_p_ds = s.cnt["ds"]
            for i in range(4):
                tt = tc * 4 + i
                kk = tt % 2
                da_p = s.dmaT("scalar", s_ptT[:, kk * 1024:(kk + 1) * 1024],
                              p_dram.ap()[:, tt * 128:(tt + 1) * 128],
                              wait=[("ds", tc_p_ds), ("dv", out_dv[kk])])
                ds_x = s.dma("sync", s_xo[:, kk * 1024:(kk + 1) * 1024],
                             x_own.ap()[tt * 128:(tt + 1) * 128, :],
                             wait=[("dv", out_dv[kk])])
                s.add("vector", lambda q, o=s_outt[:, kk * 1024:(kk + 1) * 1024],
                      a=s_xo[:, kk * 1024:(kk + 1) * 1024],
                      b=s_ptT[:, kk * 1024:(kk + 1) * 1024]:
                      q.tensor_add(o, a, b),
                      inc=[("dv", 1)], wait=[("da", da_p), ("ds", ds_x)])
                out_dv[kk] = s.cnt["dv"]
                s.dma("sync", out_own.ap()[tt * 128:(tt + 1) * 128, :],
                      s_outt[:, kk * 1024:(kk + 1) * 1024],
                      wait=[("dv", out_dv[kk])])
        if DEBUG and STAGES["B2"]:
            s.dma("sync", rsout_dbg.ap()[:, :], rs_out.ap()[:, :], wait=[("cc", 1)])
        ds_total = s.cnt["ds"]
        s.add("sync", lambda q: q.nop(), wait=[("ds", ds_total), ("da", s.cnt["da"]),
                                               ("dp", s.cnt["dp"])])
        s.emit(block)
    return nc


_NC_CACHE = {}


def kernel(L0_attn_x, L0_attn_W_attn, L0_attn_b_attn, L0_attn_W_proj, L0_attn_b_proj,
           L0_ffn_w_gate, L0_ffn_w_up, L0_ffn_w_down):
    x = np.asarray(L0_attn_x)
    W_attn = np.asarray(L0_attn_W_attn)
    b_attn = np.asarray(L0_attn_b_attn)
    W_proj = np.asarray(L0_attn_W_proj)
    b_proj = np.asarray(L0_attn_b_proj)
    wgT = np.ascontiguousarray(np.asarray(L0_ffn_w_gate).T)
    wuT = np.ascontiguousarray(np.asarray(L0_ffn_w_up).T)
    wdT = np.ascontiguousarray(np.asarray(L0_ffn_w_down).T)

    identc = np.eye(128, dtype=np.float32).astype(ml_dtypes.bfloat16)
    mw = np.zeros((128, 896), dtype=np.float32)
    r = np.arange(128)[:, None]
    cc = np.arange(896)[None, :]
    mw[(r <= cc - 384)] = 1.0
    maskwc = mw.astype(ml_dtypes.bfloat16)
    onesc = np.ones((1, 128), dtype=np.float32)
    bproj_t = np.ascontiguousarray(b_proj.reshape(8, 128).T.astype(np.float32))

    if "nc" not in _NC_CACHE:
        _NC_CACHE["nc"] = build_nc()
    nc = _NC_CACHE["nc"]

    in_maps = []
    for core in range(8):
        b, j = core // 4, core % 4
        qs = W_attn[:, 256 * j:256 * (j + 1)]
        ks = W_attn[:, 1024 + 256 * j:1024 + 256 * (j + 1)]
        vs = W_attn[:, 2048 + 256 * j:2048 + 256 * (j + 1)]
        wqkv = np.ascontiguousarray(np.concatenate([qs, ks, vs], axis=1))
        bq = np.concatenate([b_attn[256 * j:256 * (j + 1)],
                             b_attn[1024 + 256 * j:1024 + 256 * (j + 1)],
                             b_attn[2048 + 256 * j:2048 + 256 * (j + 1)]])
        bqkv_t = np.ascontiguousarray(bq.reshape(6, 128).T.astype(np.float32))
        in_maps.append({
            "x_full": np.ascontiguousarray(x[b]),
            "x_own": np.ascontiguousarray(x[b, j * TS:(j + 1) * TS]),
            "wqkv": wqkv,
            "bqkv": bqkv_t,
            "wproj": np.ascontiguousarray(W_proj[256 * j:256 * (j + 1), :]),
            "bproj": bproj_t,
            "wgT": wgT, "wuT": wuT, "wdT": wdT,
            "ident": identc, "maskw": maskwc, "ones1": onesc,
        })
    res = run_bass_kernel_spmd(nc, in_maps, core_ids=list(range(8)))
    out = np.empty((2, T, C), dtype=np.float32)
    for core in range(8):
        b, j = core // 4, core % 4
        out[b, j * TS:(j + 1) * TS] = res.results[core]["out_own"]
    return out
